# version 18 (final)
# baseline (speedup 1.0000x reference)
import contextlib
import numpy as np
import ml_dtypes
import concourse.bass as bass
import concourse.mybir as mybir
from concourse.bass_utils import run_bass_kernel_spmd

F32 = mybir.dt.float32
BF16 = mybir.dt.bfloat16
U8 = mybir.dt.uint8
AX = mybir.AxisListType.X
AF = mybir.ActivationFunctionType
ALU = mybir.AluOpType

MAGIC = float(np.float32(1.5 * 2 ** 23))
LN_EPS = 1e-5
Q_EPS = 1e-5
EXP_SHIFT = -16.0

D = 1024
I = 4096
TOK = 2048
NPT = 16
TG_PT = 8
NTG = NPT // TG_PT
NW = D * I

ENGS = ("sync", "tensor", "vector", "scalar", "gpsimd")


class Sched:

    def __init__(self):
        self.ops = {e: [] for e in ENGS}
        self.cnt = {}

    def sem(self, name):
        self.cnt[name] = 0

    def wait(self, eng, sem, val):
        if val > 0:
            self.ops[eng].append(("w", sem, val))

    def emit(self, eng, fn, incs=()):
        for s, a in incs:
            self.cnt[s] += a
        self.ops[eng].append(("e", fn, tuple(incs)))

    def vsync(self):
        self.wait("vector", "DV", self.cnt["DV"])

    def acsync(self):
        self.wait("scalar", "AC", self.cnt["AC"])

    def barrier(self):
        snap = [(s, v) for s, v in self.cnt.items() if v > 0]
        for e in ENGS:
            for s, v in snap:
                self.wait(e, s, v)

    def simulate(self):
        idx = {e: 0 for e in ENGS}
        cnt = {s: 0 for s in self.cnt}
        progress = True
        while progress:
            progress = False
            for e in ENGS:
                lst = self.ops[e]
                while idx[e] < len(lst):
                    op = lst[idx[e]]
                    if op[0] == "w":
                        if cnt[op[1]] < op[2]:
                            break
                    else:
                        for s, a in op[2]:
                            cnt[s] += a
                    idx[e] += 1
                    progress = True
        stuck = {e: idx[e] for e in ENGS if idx[e] < len(self.ops[e])}
        if stuck:
            det = []
            for e, i in stuck.items():
                op = self.ops[e][i]
                det.append(f"{e}@{i}/{len(self.ops[e])} waits {op[1]}>={op[2]}"
                           f" (have {cnt.get(op[1])})")
            raise RuntimeError("deadlock in schedule: " + "; ".join(det))

    def replay(self, eng, handle, sems):
        for op in self.ops[eng]:
            if op[0] == "w":
                handle.wait_ge(sems[op[1]], op[2])
            else:
                ins = op[1](handle)
                for s, a in op[2]:
                    ins = ins.then_inc(sems[s], a)


def build(dbg=False):
    nc = bass.Bass("TRN2", target_bir_lowering=False, debug=False)

    ap = lambda h: h.ap()
    x_in = ap(nc.dram_tensor("x", [TOK, D], F32, kind="ExternalInput"))
    wg_sh = ap(nc.dram_tensor("wg_sh", [512, 1024], F32, kind="ExternalInput"))
    wu_sh = ap(nc.dram_tensor("wu_sh", [512, 1024], F32, kind="ExternalInput"))
    wd_sh = ap(nc.dram_tensor("wd_sh", [1024, 512], F32, kind="ExternalInput"))
    vw_m = ap(nc.dram_tensor("vw_m", [512, 1024], F32, kind="ExternalInput"))
    ow_m = ap(nc.dram_tensor("ow_m", [1024, 512], F32, kind="ExternalInput"))
    vb_m = ap(nc.dram_tensor("vb_m", [1, 512], F32, kind="ExternalInput"))
    ob_in = ap(nc.dram_tensor("ob", [1, 1024], F32, kind="ExternalInput"))
    qa_in = ap(nc.dram_tensor("qa_l", [4, 8, 128, 128], F32, kind="ExternalInput"))
    ka_in = ap(nc.dram_tensor("ka_l", [4, 8, 128, 128], F32, kind="ExternalInput"))
    qb_in = ap(nc.dram_tensor("qb_l", [4, 128, 128], F32, kind="ExternalInput"))
    kb_in = ap(nc.dram_tensor("kb_l", [4, 128, 128], F32, kind="ExternalInput"))
    mask_in = ap(nc.dram_tensor("masks", [4, 128, 512], F32, kind="ExternalInput"))
    id_in = ap(nc.dram_tensor("ident", [128, 128], F32, kind="ExternalInput"))
    out_d = ap(nc.dram_tensor("out", [TOK, D], F32, kind="ExternalOutput"))

    c1_in = ap(nc.dram_tensor("c1_in", [1, 16], F32))
    c1_out = ap(nc.dram_tensor("c1_out", [1, 16], F32, addr_space="Shared"))
    wall_sh = ap(nc.dram_tensor("wall_sh", [3072, 512], BF16))
    wgqt_sh = wall_sh.tensor.ap()[0:1024, :]
    wuqt_sh = wall_sh.tensor.ap()[1024:2048, :]
    wdt_sh_flat = wall_sh.tensor.ap()[2048:3072, :]
    Gall = ap(nc.dram_tensor("Gall", [24576, 512], BF16, addr_space="Shared"))
    xat0 = ap(nc.dram_tensor("xat0", [1024, 1024], BF16))
    xat1 = ap(nc.dram_tensor("xat1", [1024, 1024], BF16))
    Gx0 = ap(nc.dram_tensor("Gx0", [2048, 1024], BF16))
    Gx1 = ap(nc.dram_tensor("Gx1", [2048, 1024], BF16))
    opart = ap(nc.dram_tensor("opart", [4096, 1024], F32))
    orsl = [ap(nc.dram_tensor(f"ors{k}", [512, 1024], F32)) for k in range(4)]
    dbg_d = {}
    if dbg:
        for nm, shp, dt in [("dbg_x1", [TOK, D], F32), ("dbg_gx", [2048, 2048], BF16),
                            ("dbg_qt", [4, 128, 4096], BF16),
                            ("dbg_kt", [4, 128, 4096], BF16),
                            ("dbg_v", [128, 16640], BF16),
                            ("dbg_ors", [2048, 1024], F32),
                            ("dbg_x2", [TOK, D], F32),
                            ("dbg_rxt", [128, 8192], BF16),
                            ("dbg_rch", [128, 1024], F32),
                            ("dbg_srow", [1, 1024], F32),
                            ("dbg_sc", [128, 64], F32),
                            ("dbg_rh", [128, 32768], BF16),
                            ("dbg_hraw", [128, 32768], BF16),
                            ("dbg_gw", [8192, 512], BF16)]:
            dbg_d[nm] = ap(nc.dram_tensor(nm, shp, dt, kind="ExternalOutput"))

    with contextlib.ExitStack() as stack:
        ecm = stack.enter_context
        S = Sched()
        semnames = ["LD", "LG", "ST", "CC", "PE", "DV", "AC", "GP",
                    "CG", "CU", "CD", "CX", "CR0", "CR1", "CR2", "CR3"]
        sems = {n: ecm(nc.semaphore(n)) for n in semnames}
        for n in semnames:
            S.sem(n)
        (LD, LG, ST, CC, PE, DV, AC, GP,
         CG, CU, CD, CX, CR0, CR1, CR2, CR3) = semnames
        CRk = [CR0, CR1, CR2, CR3]

        P01 = ecm(nc.psum_tensor("P01", [128, 1024], F32))
        P23 = ecm(nc.psum_tensor("P23", [128, 1024], F32))
        P45 = ecm(nc.psum_tensor("P45", [128, 1024], F32))
        P6 = ecm(nc.psum_tensor("P6", [128, 512], F32))
        P7b = ecm(nc.psum_tensor("P7b", [128, 512], BF16))

        stream = nc.alloc_sbuf_tensor("stream", [128, NPT * 1024], F32)
        ident_b = nc.alloc_sbuf_tensor("ident_b", [128, 128], BF16)
        ident_f = nc.alloc_sbuf_tensor("ident_f", [128, 128], F32)
        ones_b = nc.alloc_sbuf_tensor("ones_b", [1, 128], BF16)
        half_b = nc.alloc_sbuf_tensor("half_b", [1, 128], BF16)
        ones_f = nc.alloc_sbuf_tensor("ones_f", [1, 128], F32)
        onecol_f = nc.alloc_sbuf_tensor("onecol_f", [128, 1], F32)
        vbrow = nc.alloc_sbuf_tensor("vbrow", [1, 512], BF16)
        obrow = nc.alloc_sbuf_tensor("obrow", [1, 1024], BF16)
        sc = nc.alloc_sbuf_tensor("sc", [128, NPT * 4], F32)
        wmean = nc.alloc_sbuf_tensor("wmean", [128, 8], F32)
        wrow = nc.alloc_sbuf_tensor("wrow", [1, 16], F32)
        smallf = nc.alloc_sbuf_tensor("smallf", [128, 16], F32)
        lnsc = nc.alloc_sbuf_tensor("lnsc", [128, 160], F32)
        qv = nc.alloc_sbuf_tensor("qv", [128, 64], F32)
        epsb = nc.alloc_sbuf_tensor("epsb", [128, 1], F32)
        shiftb = nc.alloc_sbuf_tensor("shiftb", [128, 1], F32)

        AR_BYTES = 139776
        arena = nc.alloc_sbuf_tensor("arena", [128, AR_BYTES], U8)
        ar0 = nc.lookup_mloc(arena).addr

        def at(name, shape, dtype, off):
            sz = {BF16: 2, F32: 4, U8: 1}[dtype]
            nbytes = int(np.prod(shape[1:])) * sz
            assert off >= 0 and off + nbytes <= AR_BYTES, \
                f"{name} overflows arena: {off}+{nbytes}"
            return nc.alloc_sbuf_tensor_at(name, shape, dtype, offset=ar0 + off)

        hF = at("hF", [128, 32 * 1024], BF16, 0)
        rxT = at("rxT", [128, 8 * 1024], BF16, 65536)
        wgb = [at(f"wgb{i}", [128, 8 * 128], BF16, 81920 + 2048 * i) for i in range(2)]
        wub = [at(f"wub{i}", [128, 8 * 128], BF16, 86016 + 2048 * i) for i in range(2)]
        srow = at("srow", [1, 1024], F32, 90112)
        wdb = at("wdb", [128, 32 * 512], BF16, 90112)
        rxs = at("rxs", [128, 1024], BF16, 122880)
        Rneg = at("Rneg", [128, 1024], F32, 122880)
        xq = at("xq", [128, 1024], F32, 126976)
        q1 = at("q1", [128, 1024], F32, 126976)
        Rch = at("Rch", [128, 1024], F32, 126976)
        q1b = at("q1b", [128, 1024], F32, 131072)
        rsc = [at(f"rsc{i}", [128, 512], F32, 131072 + 2048 * i) for i in range(2)]
        tsc = [at(f"tsc{i}", [128, 512], F32, 135168 + 2048 * i) for i in range(2)]
        wsh = at("wsh", [128, 4096], F32, 0)
        wq2 = at("wq2", [128, 4096], F32, 16384)
        wqb = at("wqb", [128, 4096], BF16, 32768)
        qts = at("qts", [128, 4096], BF16, 40960)
        xats = at("xats", [128, 8 * 2048], BF16, 0)
        xhat3 = at("xhat3", [128, 1024], BF16, 32768)
        QT = at("QT", [128, 4 * 4096], BF16, 0)
        KT = at("KT", [128, 4 * 4096], BF16, 32768)
        VA = at("VA", [128, 32 * 520], BF16, 65536)
        vwraw = at("vwraw", [128, 4 * 1024], BF16, 98816)
        vwT = at("vwT", [128, 8 * 512], BF16, 107008)
        xw = [at(f"xw{i}", [128, 8 * 512], BF16, 115200 + 8192 * i) for i in range(2)]
        qa_sb = at("qa_sb", [128, 32 * 128], BF16, 98816)
        ka_sb = at("ka_sb", [128, 32 * 128], BF16, 107008)
        qb_sb = at("qb_sb", [128, 4 * 128], BF16, 131584)
        kb_sb = at("kb_sb", [128, 4 * 128], BF16, 132608)
        mids = [at(f"mids{i}", [128, 512], BF16, 133632 + 1024 * i) for i in range(4)]
        msk = at("msk", [128, 4 * 512], BF16, 98816)
        owT = at("owT", [128, 4 * 1024], BF16, 102912)
        pbuf = [at(f"pbuf{i}", [128, 2048], BF16, 111104 + 4096 * i) for i in range(2)]
        ybuf = at("ybuf", [128, 4 * 512], BF16, 119296)
        opst = [at(f"opst{i}", [128, 1024], F32, 123392 + 4096 * i) for i in range(2)]
        nsc = at("nsc", [128, 1024], F32, 131584)
        owraw = at("owraw", [128, 8 * 512], BF16, 119296)
        xo = [at(f"xo{i}", [128, 1024], F32, 4096 * i) for i in range(2)]
        xoa = at("xoa", [128, 1024], F32, 135680)
        xo2 = [at(f"xo2_{i}", [128, 1024], F32, 65536 + 4096 * i) for i in range(2)]

        def R8(t):
            return t[:, :].rearrange("p (dc c) -> p dc c", dc=8)

        xt = x_in.rearrange("(n p) m -> n p m", p=128)
        for pt in range(NPT):
            S.emit("sync", lambda e, pt=pt: e.dma_start(
                out=stream[:, 1024 * pt:1024 * (pt + 1)], in_=xt[pt]), [(LD, 16)])
        S.emit("sync", lambda e: e.dma_start(out=ident_f[:], in_=id_in), [(LD, 16)])
        S.emit("gpsimd", lambda e: e.dma_start(out=ident_b[:], in_=id_in), [(LG, 16)])

        S.emit("vector", lambda e: e.memset(ones_b[:], 1.0), [(DV, 1)])
        S.emit("vector", lambda e: e.memset(half_b[:], 0.5), [(DV, 1)])
        S.emit("vector", lambda e: e.memset(ones_f[:], 1.0), [(DV, 1)])
        S.emit("vector", lambda e: e.memset(onecol_f[:], 1.0), [(DV, 1)])
        S.emit("vector", lambda e: e.memset(epsb[:], LN_EPS), [(DV, 1)])
        S.emit("vector", lambda e: e.memset(shiftb[:], EXP_SHIFT), [(DV, 1)])

        S.emit("vector", lambda e: e.memset(wrow[:], 0.0), [(DV, 1)])
        wsrc = [(wg_sh, 4, 1024, wgqt_sh), (wu_sh, 4, 1024, wuqt_sh),
                (wd_sh, 8, 512, wdt_sh_flat)]
        for wi, (src, s_, f_, dst) in enumerate(wsrc):
            src_t = src.rearrange("(s p) f -> p s f", p=128)
            S.emit("sync", lambda e, src_t=src_t, s_=s_, f_=f_: e.dma_start(
                out=wsh[:, 0:s_ * f_].rearrange("p (s f) -> p s f", s=s_), in_=src_t),
                [(LD, 16)])
            if wi == 0:
                ld_x = S.cnt[LD]
            S.wait("vector", LD, S.cnt[LD])
            S.emit("vector", lambda e, s_=s_, f_=f_: e.reduce_sum(
                out=smallf[:, 0:s_], in_=wsh[:, 0:s_ * f_].rearrange(
                    "p (s f) -> p s f", s=s_), axis=AX, apply_absolute_value=True),
                [(DV, 1)])
            S.wait("vector", DV, S.cnt[DV])
            S.emit("vector", lambda e, s_=s_: e.reduce_sum(
                out=smallf[:, 8:9], in_=smallf[:, 0:s_], axis=AX), [(DV, 1)])
            S.wait("tensor", DV, S.cnt[DV])
            S.emit("tensor", lambda e, wi=wi: e.matmul(
                out=P6[0:1, wi:wi + 1], lhsT=smallf[:, 8:9], rhs=onecol_f[:],
                start=True, stop=True), [(PE, 1)])
            S.wait("vector", PE, S.cnt[PE])
            S.emit("vector", lambda e, wi=wi: e.tensor_copy(
                out=wrow[0:1, wi:wi + 1], in_=P6[0:1, wi:wi + 1]), [(DV, 1)])
            S.wait("sync", DV, S.cnt[DV])
        S.wait("gpsimd", DV, S.cnt[DV])
        S.emit("gpsimd", lambda e: e.dma_start(out=c1_in, in_=wrow[:]), [(ST, 16)])
        S.wait("gpsimd", ST, S.cnt[ST])
        S.emit("gpsimd", lambda e: e.collective_compute(
            "AllReduce", ALU.add, replica_groups=[list(range(8))],
            ins=[c1_in], outs=[c1_out]), [(CC, 1)])
        S.wait("sync", DV, S.cnt[DV])
        src0_t = wsrc[0][0].rearrange("(s p) f -> p s f", p=128)
        S.emit("sync", lambda e, src0_t=src0_t: e.dma_start(
            out=wsh[:, 0:4096].rearrange("p (s f) -> p s f", s=4), in_=src0_t),
            [(LD, 16)])
        S.wait("sync", CC, S.cnt[CC])
        S.emit("sync", lambda e: e.dma_start(out=wrow[:], in_=c1_out), [(LD, 16)])
        S.wait("vector", LD, S.cnt[LD])
        S.emit("vector", lambda e: e.tensor_scalar(
            out=wrow[0:1, 0:3], in0=wrow[0:1, 0:3], scalar1=1.0 / NW, scalar2=Q_EPS,
            op0=ALU.mult, op1=ALU.max), [(DV, 1)])
        S.vsync()
        S.wait("tensor", DV, S.cnt[DV])
        S.emit("tensor", lambda e: e.matmul(
            out=P6[:, 0:3], lhsT=ones_f[0:1, :], rhs=wrow[0:1, 0:3],
            start=True, stop=True), [(PE, 1)])
        S.wait("vector", PE, S.cnt[PE])
        S.emit("vector", lambda e: e.tensor_copy(out=wmean[:, 0:3], in_=P6[:, 0:3]),
               [(DV, 1)])
        S.vsync()
        S.emit("vector", lambda e: e.reciprocal(out=wmean[:, 3:6], in_=wmean[:, 0:3]),
               [(DV, 1)])
        S.emit("vector", lambda e: e.tensor_mul(
            out=wmean[:, 6:7], in0=wmean[:, 0:1], in1=wmean[:, 0:1]), [(DV, 1)])
        S.vsync()
        S.emit("vector", lambda e: e.tensor_mul(
            out=wmean[:, 6:7], in0=wmean[:, 6:7], in1=wmean[:, 1:2]), [(DV, 1)])
        S.vsync()
        v_mw = S.cnt[DV]

        S.wait("tensor", LG, 16)
        for wi, (src, s_, f_, dst) in enumerate(wsrc):
            n_el = s_ * f_
            ndc = f_ // 128
            src_t = src.rearrange("(s p) f -> p s f", p=128)
            if wi > 0:
                S.wait("sync", DV, S.cnt[DV])
                S.emit("sync", lambda e, src_t=src_t, s_=s_, n_el=n_el: e.dma_start(
                    out=wsh[:, 0:n_el].rearrange("p (s f) -> p s f", s=s_),
                    in_=src_t), [(LD, 16)])
            S.wait("vector", LD, S.cnt[LD])
            S.wait("vector", DV, v_mw)
            S.emit("vector", lambda e, n_el=n_el, wi=wi: e.tensor_scalar(
                out=wq2[:, 0:n_el], in0=wsh[:, 0:n_el], scalar1=wmean[:, 3 + wi:4 + wi],
                scalar2=MAGIC, op0=ALU.mult, op1=ALU.add), [(DV, 1)])
            S.vsync()
            S.emit("vector", lambda e, n_el=n_el: e.tensor_scalar(
                out=wq2[:, 0:n_el], in0=wq2[:, 0:n_el], scalar1=MAGIC, scalar2=1.0,
                op0=ALU.subtract, op1=ALU.min), [(DV, 1)])
            S.vsync()
            S.emit("vector", lambda e, n_el=n_el: e.tensor_scalar(
                out=wqb[:, 0:n_el], in0=wq2[:, 0:n_el], scalar1=-1.0, scalar2=None,
                op0=ALU.max), [(DV, 1)])
            S.vsync()
            v_q = S.cnt[DV]
            S.wait("vector", ST, S.cnt[ST])
            for s in range(s_):
                for dcq in range(0, ndc, 4):
                    nb = min(4, ndc - dcq)
                    S.wait("tensor", DV, S.cnt[DV])
                    for j in range(nb):
                        S.emit("tensor", lambda e, s=s, dcq=dcq, j=j, f_=f_: e.transpose(
                            out=P7b[:, 128 * j:128 * (j + 1)],
                            in_=wqb[:, f_ * s + 128 * (dcq + j):f_ * s + 128 * (dcq + j + 1)],
                            identity=ident_b[:]),
                            [(PE, 1)] if j == nb - 1 else [])
                    S.wait("vector", PE, S.cnt[PE])
                    S.emit("vector", lambda e, s=s, dcq=dcq, nb=nb, s_=s_, ndc=ndc:
                           e.tensor_copy(
                               out=qts[:, :].rearrange("p (dc c) -> p dc c", dc=ndc)[
                                   :, dcq:dcq + nb, 128 * s:128 * (s + 1)],
                               in_=P7b[:, 0:128 * nb].rearrange(
                                   "p (j c) -> p j c", j=nb)), [(DV, 1)])
            v_t = S.cnt[DV]
            if f_ == 1024:
                dst_t = dst.rearrange("(dc p) f -> dc p f", p=128)
            else:
                dst_t = dst.rearrange("(dc p two) f -> dc p (two f)", p=128, two=2)
            S.wait("gpsimd", DV, v_t)
            for dc in range(ndc):
                S.emit("gpsimd", lambda e, dc=dc, s_=s_, ndc=ndc, dst_t=dst_t:
                       e.dma_start(out=dst_t[dc],
                                   in_=qts[:, :].rearrange("p (dc c) -> p dc c", dc=ndc)[
                                       :, dc, 0:128 * s_]), [(ST, 16)])
        S.wait("gpsimd", ST, S.cnt[ST])
        S.emit("gpsimd", lambda e: e.collective_compute(
            "AllGather", ALU.bypass, replica_groups=[list(range(8))],
            ins=[wall_sh], outs=[Gall]), [(CG, 1)])

        def emit_ln(pt, out_ap):
            xs = stream[:, 1024 * pt:1024 * (pt + 1)]
            S.wait("vector", DV, S.cnt[DV])
            S.emit("vector", lambda e: e.bn_stats(out=smallf[:, 0:6], in_=xs[:, 0:512]),
                   [(DV, 1)])
            S.emit("vector", lambda e: e.bn_stats(out=smallf[:, 6:12], in_=xs[:, 512:1024]),
                   [(DV, 1)])
            S.wait("vector", DV, S.cnt[DV])
            S.emit("vector", lambda e: e.bn_aggr(out=smallf[:, 12:14], in_=smallf[:, 0:12]),
                   [(DV, 1)])
            S.wait("scalar", DV, S.cnt[DV])
            S.emit("scalar", lambda e: e.activation(
                out=smallf[:, 14:15], in_=smallf[:, 13:14], func=AF.Ln, bias=epsb[:]),
                [(AC, 1)])
            S.acsync()
            S.emit("scalar", lambda e: e.activation(
                out=smallf[:, 14:15], in_=smallf[:, 14:15], func=AF.Exp, scale=-0.5),
                [(AC, 1)])
            S.acsync()
            S.wait("vector", AC, S.cnt[AC])
            S.vsync()
            S.emit("vector", lambda e: e.scalar_tensor_tensor(
                out=smallf[:, 15:16], in0=smallf[:, 12:13], scalar=-1.0,
                in1=smallf[:, 14:15], op0=ALU.mult, op1=ALU.mult), [(DV, 1)])
            S.wait("scalar", DV, S.cnt[DV])
            S.emit("scalar", lambda e, out_ap=out_ap: e.activation(
                out=out_ap, in_=xs, func=AF.Identity,
                bias=smallf[:, 15:16], scale=smallf[:, 14:15]), [(AC, 1)])
            return S.cnt[AC]

        stream_done = {}

        def emit_ffn(inv, mid_hook=None, pre_ld=None):
            gv = Gall.rearrange("(r w dc p) f -> w r p dc f", w=3, dc=8, p=128)
            gwt = gv[0]
            uwt = gv[1]
            wdt_t = Gall.rearrange("(r w icl p two) f -> w two r p icl f",
                                   w=3, icl=4, p=128, two=2)[2]
            LLv = lnsc[:, :].rearrange("p (l c) -> p l c", c=16)
            scv = sc[:, :].rearrange("p (l c) -> p l c", c=4)

            def ln_passes(tg, pe_guard):
                pts = list(range(tg * TG_PT, (tg + 1) * TG_PT))
                S.wait("vector", PE, pe_guard)
                S.wait("vector", LD, ld_x)
                S.vsync()
                for li, pt in enumerate(pts):
                    c0 = 16 * li
                    xs = stream[:, 1024 * pt:1024 * (pt + 1)]
                    S.emit("vector", lambda e, c0=c0, xs=xs: e.bn_stats(
                        out=lnsc[:, c0:c0 + 6], in_=xs[:, 0:512]), [(DV, 1)])
                    S.emit("vector", lambda e, c0=c0, xs=xs: e.bn_stats(
                        out=lnsc[:, c0 + 6:c0 + 12], in_=xs[:, 512:1024]), [(DV, 1)])
                    S.emit("vector", lambda e, c0=c0, xs=xs: e.tensor_reduce(
                        out=lnsc[:, c0 + 12:c0 + 13], in_=xs, axis=AX, op=ALU.max),
                        [(DV, 1)])
                    S.emit("vector", lambda e, c0=c0, xs=xs: e.tensor_reduce(
                        out=lnsc[:, c0 + 13:c0 + 14], in_=xs, axis=AX, op=ALU.min),
                        [(DV, 1)])
                S.vsync()
                for li in range(8):
                    c0 = 16 * li
                    S.emit("vector", lambda e, c0=c0: e.bn_aggr(
                        out=lnsc[:, c0 + 14:c0 + 16], in_=lnsc[:, c0:c0 + 12]),
                        [(DV, 1)])
                S.vsync()
                S.wait("scalar", DV, S.cnt[DV])
                S.emit("scalar", lambda e: e.activation(
                    out=smallf[:, 0:8], in_=LLv[:, 0:8, 15:16], func=AF.Ln,
                    bias=epsb[:]), [(AC, 1)])
                S.acsync()
                S.emit("scalar", lambda e: e.activation(
                    out=smallf[:, 0:8], in_=smallf[:, 0:8], func=AF.Exp, scale=-0.5),
                    [(AC, 1)])
                S.acsync()
                S.wait("vector", AC, S.cnt[AC])
                S.emit("vector", lambda e: e.tensor_sub(
                    out=qv[:, 0:8], in0=LLv[:, 0:8, 12:13], in1=LLv[:, 0:8, 14:15]),
                    [(DV, 1)])
                S.emit("vector", lambda e: e.tensor_sub(
                    out=qv[:, 8:16], in0=LLv[:, 0:8, 14:15], in1=LLv[:, 0:8, 13:14]),
                    [(DV, 1)])
                S.vsync()
                S.emit("vector", lambda e: e.tensor_max(
                    out=qv[:, 16:24], in0=qv[:, 0:8], in1=qv[:, 8:16]), [(DV, 1)])
                S.vsync()
                S.emit("vector", lambda e: e.tensor_mul(
                    out=qv[:, 24:32], in0=qv[:, 16:24], in1=smallf[:, 0:8]), [(DV, 1)])
                S.vsync()
                S.emit("vector", lambda e: e.tensor_scalar(
                    out=qv[:, 24:32], in0=qv[:, 24:32], scalar1=Q_EPS, scalar2=None,
                    op0=ALU.max), [(DV, 1)])
                S.vsync()
                S.emit("vector", lambda e, tg=tg: e.tensor_copy(
                    out=scv[:, 8 * tg:8 * tg + 8, 0:1], in_=qv[:, 24:32]), [(DV, 1)])
                S.emit("vector", lambda e: e.reciprocal(
                    out=qv[:, 48:56], in_=qv[:, 24:32]), [(DV, 1)])
                S.vsync()
                S.emit("vector", lambda e: e.tensor_mul(
                    out=qv[:, 32:40], in0=smallf[:, 0:8], in1=qv[:, 48:56]), [(DV, 1)])
                S.vsync()
                S.emit("vector", lambda e: e.tensor_scalar(
                    out=qv[:, 32:40], in0=qv[:, 32:40], scalar1=127.0, scalar2=None,
                    op0=ALU.mult), [(DV, 1)])
                S.vsync()
                S.emit("vector", lambda e: e.scalar_tensor_tensor(
                    out=qv[:, 40:48], in0=LLv[:, 0:8, 14:15], scalar=-1.0,
                    in1=qv[:, 32:40], op0=ALU.mult, op1=ALU.mult), [(DV, 1)])
                S.vsync()

            def ln_pass4(tg, pe_guard):
                pts = list(range(tg * TG_PT, (tg + 1) * TG_PT))
                for li, pt in enumerate(pts):
                    xs = stream[:, 1024 * pt:1024 * (pt + 1)]
                    if li == 0:
                        S.wait("vector", PE, pe_guard)
                    else:
                        S.wait("vector", PE, S.cnt[PE])
                    S.emit("vector", lambda e, xs=xs, li=li: e.tensor_scalar(
                        out=xq[:], in0=xs, scalar1=qv[:, 32 + li:33 + li],
                        scalar2=qv[:, 40 + li:41 + li], op0=ALU.mult, op1=ALU.add),
                        [(DV, 1)])
                    S.vsync()
                    S.emit("vector", lambda e: e.tensor_scalar(
                        out=rxs[:], in0=xq[:], scalar1=MAGIC, scalar2=MAGIC,
                        op0=ALU.add, op1=ALU.subtract), [(DV, 1)])
                    v_rx = S.cnt[DV]
                    for bq in range(2):
                        S.wait("tensor", DV, S.cnt[DV])
                        for j in range(4):
                            S.emit("tensor", lambda e, bq=bq, j=j: e.transpose(
                                out=P7b[:, 128 * j:128 * (j + 1)],
                                in_=rxs[:, 128 * (4 * bq + j):128 * (4 * bq + j + 1)],
                                identity=ident_b[:]), [(PE, 1)] if j == 3 else [])
                        S.wait("vector", PE, S.cnt[PE])
                        S.emit("vector", lambda e, bq=bq, li=li: e.tensor_copy(
                            out=R8(rxT)[:, 4 * bq:4 * bq + 4, 128 * li:128 * (li + 1)],
                            in_=P7b[:, 0:512].rearrange("p (j c) -> p j c", j=4)),
                            [(DV, 1)])
                return S.cnt[DV]

            pe_sbc = S.cnt[PE]
            ln_passes(0, S.cnt[PE])
            v_rxT = ln_pass4(0, S.cnt[PE])
            chain_last = {}
            hpipe = {}
            for tg in range(NTG):
                pts = list(range(tg * TG_PT, (tg + 1) * TG_PT))

                if inv == 0 and tg == 0:
                    S.wait("sync", CG, 1)
                S.wait("vector", PE, S.cnt[PE])
                S.wait("gpsimd", PE, S.cnt[PE])
                S.emit("vector", lambda e: e.memset(Rch[:], 0.0), [(DV, 1)])
                S.emit("vector", lambda e: e.memset(Rneg[:], 0.0), [(DV, 1)])
                wbuf_free = {}
                ld_c32 = {}

                def load_wtile(c32):
                    r, jj = c32 // 4, c32 % 4
                    par = c32 % 2
                    S.wait("sync", PE, wbuf_free.get(par, 0))
                    S.emit("sync", lambda e, r=r, jj=jj, par=par: e.dma_start(
                        out=R8(wgb[par]), in_=gwt[r, :, :, 128 * jj:128 * (jj + 1)]),
                        [(LD, 16)])
                    S.emit("sync", lambda e, r=r, jj=jj, par=par: e.dma_start(
                        out=R8(wub[par]), in_=uwt[r, :, :, 128 * jj:128 * (jj + 1)]),
                        [(LD, 16)])
                    ld_c32[c32] = S.cnt[LD]

                if tg == 0 and pre_ld is not None:
                    ld_c32.update(pre_ld)
                else:
                    load_wtile(0)
                    load_wtile(1)
                for c32 in range(32):
                    par = c32 % 2
                    ld_w = ld_c32[c32]
                    for tt in range(2):
                        gslc = P01[:, 512 * tt:512 * (tt + 1)]
                        uslc = P23[:, 512 * tt:512 * (tt + 1)]
                        S.wait("tensor", LD, ld_w)
                        S.wait("tensor", DV, v_rxT)
                        key = (tg * 32 + c32 - 1, tt)
                        if key in hpipe:
                            S.wait("tensor", AC, hpipe[key][0])
                            S.wait("tensor", DV, hpipe[key][1])
                        for dc in range(8):
                            S.emit("tensor", lambda e, par=par, dc=dc, tt=tt, gslc=gslc:
                                   e.matmul(out=gslc,
                                            lhsT=wgb[par][:, 128 * dc:128 * (dc + 1)],
                                            rhs=R8(rxT)[:, dc, 512 * tt:512 * (tt + 1)],
                                            start=(dc == 0), stop=(dc == 7)),
                                   [(PE, 1)] if dc == 7 else [])
                        for dc in range(8):
                            S.emit("tensor", lambda e, par=par, dc=dc, tt=tt, uslc=uslc:
                                   e.matmul(out=uslc,
                                            lhsT=wub[par][:, 128 * dc:128 * (dc + 1)],
                                            rhs=R8(rxT)[:, dc, 512 * tt:512 * (tt + 1)],
                                            start=(dc == 0), stop=(dc == 7)),
                                   [(PE, 1)] if dc == 7 else [])
                        pe_mm = S.cnt[PE]
                        S.wait("scalar", PE, pe_mm)
                        S.wait("scalar", DV, S.cnt[DV])
                        S.wait("scalar", GP, S.cnt[GP])
                        S.emit("scalar", lambda e, tt=tt, gslc=gslc: e.activation(
                            out=rsc[tt][:], in_=gslc, func=AF.Relu), [(AC, 1)])
                        a_r = S.cnt[AC]
                        S.wait("vector", AC, a_r)
                        S.wait("vector", GP, S.cnt[GP])
                        S.emit("vector", lambda e, tt=tt, uslc=uslc: e.scalar_tensor_tensor(
                            out=tsc[tt][:], in0=rsc[tt][:], scalar=0.0, in1=uslc,
                            op0=ALU.bypass, op1=ALU.mult), [(DV, 1)])
                        v_t = S.cnt[DV]
                        S.wait("gpsimd", DV, v_t)
                        S.emit("gpsimd", lambda e, tt=tt, c32=c32: e.tensor_tensor(
                            out=hF[:, 1024 * c32 + 512 * tt:1024 * c32 + 512 * (tt + 1)],
                            in0=rsc[tt][:], in1=tsc[tt][:], op=ALU.mult), [(GP, 1)])
                        S.wait("vector", GP, S.cnt[GP])
                        if ("p", tt) in chain_last:
                            S.wait("vector", DV, chain_last[("p", tt)])
                        S.emit("vector", lambda e, tt=tt, c32=c32: e.tensor_max(
                            out=Rch[:, 512 * tt:512 * (tt + 1)],
                            in0=Rch[:, 512 * tt:512 * (tt + 1)],
                            in1=hF[:, 1024 * c32 + 512 * tt:1024 * c32 + 512 * (tt + 1)]),
                            [(DV, 1)])
                        chain_last[("p", tt)] = S.cnt[DV]
                        if ("n", tt) in chain_last:
                            S.wait("vector", DV, chain_last[("n", tt)])
                        S.emit("vector", lambda e, tt=tt, c32=c32: e.scalar_tensor_tensor(
                            out=Rneg[:, 512 * tt:512 * (tt + 1)],
                            in0=hF[:, 1024 * c32 + 512 * tt:1024 * c32 + 512 * (tt + 1)],
                            scalar=-1.0,
                            in1=Rneg[:, 512 * tt:512 * (tt + 1)],
                            op0=ALU.mult, op1=ALU.max), [(DV, 1)])
                        chain_last[("n", tt)] = S.cnt[DV]
                        hpipe[(tg * 32 + c32, tt)] = (a_r, v_t)
                    wbuf_free[par] = S.cnt[PE]
                    if c32 + 2 < 32:
                        load_wtile(c32 + 2)
                S.vsync()
                S.emit("vector", lambda e: e.tensor_max(
                    out=Rch[:], in0=Rch[:], in1=Rneg[:]), [(DV, 1)])
                S.vsync()
                v_chain = S.cnt[DV]
                if dbg and inv == 0 and tg == 0:
                    S.wait("gpsimd", DV, S.cnt[DV])
                    S.wait("gpsimd", GP, S.cnt[GP])
                    S.emit("gpsimd", lambda e: e.dma_start(
                        out=dbg_d["dbg_hraw"], in_=hF[:]), [(ST, 16)])
                    S.wait("vector", ST, S.cnt[ST])

                S.wait("tensor", DV, v_chain)
                for li in range(8):
                    S.emit("tensor", lambda e, li=li: e.transpose(
                        out=P45[:, 128 * li:128 * (li + 1)],
                        in_=Rch[:, 128 * li:128 * (li + 1)],
                        identity=ident_f[:]), [(PE, 1)] if li == 7 else [])
                S.wait("vector", PE, S.cnt[PE])
                for li in range(8):
                    S.emit("vector", lambda e, li=li: e.reduce_max(
                        out=qv[:, li:li + 1], in_=P45[:, 128 * li:128 * (li + 1)],
                        axis=AX), [(DV, 1)])
                S.vsync()
                S.emit("vector", lambda e, tg=tg: e.tensor_scalar(
                    out=qv[:, 8:16], in0=scv[:, 8 * tg:8 * tg + 8, 0:1],
                    scalar1=1.0 / 127.0, scalar2=None, op0=ALU.mult), [(DV, 1)])
                S.vsync()
                S.emit("vector", lambda e: e.tensor_mul(
                    out=qv[:, 16:24], in0=qv[:, 8:16], in1=qv[:, 8:16]), [(DV, 1)])
                S.vsync()
                S.emit("vector", lambda e: e.tensor_mul(
                    out=qv[:, 16:24], in0=qv[:, 16:24], in1=qv[:, 8:16]), [(DV, 1)])
                S.vsync()
                S.emit("vector", lambda e: e.tensor_scalar(
                    out=qv[:, 24:32], in0=qv[:, 16:24], scalar1=wmean[:, 6:7],
                    scalar2=None, op0=ALU.mult), [(DV, 1)])
                S.vsync()
                S.emit("vector", lambda e: e.reciprocal(
                    out=qv[:, 32:40], in_=qv[:, 24:32]), [(DV, 1)])
                S.vsync()
                S.emit("vector", lambda e: e.tensor_scalar(
                    out=qv[:, 32:40], in0=qv[:, 32:40], scalar1=Q_EPS, scalar2=None,
                    op0=ALU.mult), [(DV, 1)])
                S.vsync()
                S.emit("vector", lambda e: e.tensor_max(
                    out=qv[:, 40:48], in0=qv[:, 0:8], in1=qv[:, 32:40]), [(DV, 1)])
                S.vsync()
                S.emit("vector", lambda e: e.reciprocal(
                    out=qv[:, 48:56], in_=qv[:, 40:48]), [(DV, 1)])
                S.emit("vector", lambda e: e.tensor_mul(
                    out=qv[:, 56:64], in0=qv[:, 24:32], in1=qv[:, 40:48]), [(DV, 1)])
                S.vsync()
                S.emit("vector", lambda e: e.tensor_scalar(
                    out=qv[:, 48:56], in0=qv[:, 48:56], scalar1=127.0, scalar2=None,
                    op0=ALU.mult), [(DV, 1)])
                S.emit("vector", lambda e, tg=tg: e.tensor_scalar(
                    out=scv[:, 8 * tg:8 * tg + 8, 1:2], in0=qv[:, 56:64],
                    scalar1=wmean[:, 2:3], scalar2=0.5 / 127.0, op0=ALU.mult,
                    op1=ALU.mult), [(DV, 1)])
                S.vsync()
                for rnd in range(2):
                    S.wait("tensor", DV, S.cnt[DV])
                    for j in range(4):
                        S.emit("tensor", lambda e, rnd=rnd, j=j: e.transpose(
                            out=P6[0:1, 128 * j:128 * (j + 1)],
                            in_=qv[:, 48 + 4 * rnd + j:49 + 4 * rnd + j],
                            identity=ident_f[:]), [(PE, 1)] if j == 3 else [])
                    S.wait("vector", PE, S.cnt[PE])
                    S.emit("vector", lambda e, rnd=rnd: e.tensor_copy(
                        out=srow[0:1, 512 * rnd:512 * (rnd + 1)], in_=P6[0:1, 0:512]),
                        [(DV, 1)])
                v_srow = S.cnt[DV]
                S.wait("tensor", DV, v_srow)
                for half in range(2):
                    S.emit("tensor", lambda e, half=half: e.matmul(
                        out=P45[:, 512 * half:512 * (half + 1)], lhsT=ones_f[0:1, :],
                        rhs=srow[0:1, 512 * half:512 * (half + 1)], start=True,
                        stop=True), [(PE, 1)] if half else [])
                pe_sbc = S.cnt[PE]

                S.wait("vector", PE, pe_sbc)
                qb_ = [q1, q1b]
                stt_done = {}
                ts_done = {}
                rh_done = {}
                for step in range(33):
                    if step < 32:
                        b = step % 2
                        if b in ts_done:
                            S.wait("vector", DV, ts_done[b])
                        S.emit("vector", lambda e, c32=step, b=b: e.scalar_tensor_tensor(
                            out=qb_[b][:], in0=hF[:, 1024 * c32:1024 * (c32 + 1)],
                            scalar=0.0, in1=P45[:, 0:1024], op0=ALU.bypass,
                            op1=ALU.mult), [(DV, 1)])
                        stt_done[b] = S.cnt[DV]
                    if step >= 1:
                        c32 = step - 1
                        b = c32 % 2
                        S.wait("vector", DV, stt_done[b])
                        S.emit("vector", lambda e, c32=c32, b=b: e.tensor_scalar(
                            out=hF[:, 1024 * c32:1024 * (c32 + 1)], in0=qb_[b][:],
                            scalar1=MAGIC, scalar2=MAGIC, op0=ALU.add,
                            op1=ALU.subtract), [(DV, 1)])
                        ts_done[b] = S.cnt[DV]
                        rh_done[c32] = S.cnt[DV]
                v_rh = S.cnt[DV]
                if dbg and inv == 0 and tg == 0:
                    S.wait("gpsimd", DV, S.cnt[DV])
                    S.emit("gpsimd", lambda e: e.dma_start(
                        out=dbg_d["dbg_rxt"], in_=rxT[:]), [(ST, 16)])
                    S.emit("gpsimd", lambda e: e.dma_start(
                        out=dbg_d["dbg_rch"], in_=Rch[:]), [(ST, 16)])
                    S.emit("gpsimd", lambda e: e.dma_start(
                        out=dbg_d["dbg_sc"], in_=sc[:]), [(ST, 16)])
                    S.emit("gpsimd", lambda e: e.dma_start(
                        out=dbg_d["dbg_rh"], in_=hF[:]), [(ST, 16)])
                    for r_ in range(8):
                        S.emit("gpsimd", lambda e, r_=r_: e.dma_start(
                            out=dbg_d["dbg_gw"].rearrange(
                                "(r x) f -> r x f", x=1024)[r_],
                            in_=Gall.rearrange(
                                "(r w x) f -> r w x f", w=3, x=1024)[r_, 0]),
                            [(ST, 16)])
                    st_dbg = S.cnt[ST]
                    S.wait("tensor", ST, st_dbg)

                if inv == 0 and tg == 0:
                    S.wait("sync", CG, 1)
                S.wait("sync", PE, pe_sbc)
                for rr in range(8):
                    S.emit("sync", lambda e, rr=rr: e.dma_start(
                        out=wdb[:, :].rearrange("p (r icl c) -> p r icl c",
                                                r=8, icl=4)[:, rr],
                        in_=wdt_t[0, rr]), [(LD, 16)])
                ld_wd0 = S.cnt[LD]

                if tg == 0 and mid_hook is not None:
                    mid_hook(pe_sbc)

                if tg + 1 < NTG:
                    ln_passes(tg + 1, pe_sbc)

                obank4 = [P6[:, 0:512], P01[:, 0:512], P01[:, 512:1024],
                          P23[:, 0:512]]
                oguard = {}
                for mh in range(2):
                    if mh:
                        S.wait("sync", PE, S.cnt[PE])
                        for rr in range(8):
                            S.emit("sync", lambda e, rr=rr: e.dma_start(
                                out=wdb[:, :].rearrange("p (r icl c) -> p r icl c",
                                                        r=8, icl=4)[:, rr],
                                in_=wdt_t[1, rr]), [(LD, 16)])
                    ld_wd = S.cnt[LD] if mh else ld_wd0

                    def emit_ostt(li, pt, ob, mh):
                        S.emit("vector", lambda e, pt=pt, mh=mh, ob=ob:
                               e.scalar_tensor_tensor(
                                   out=stream[:, 1024 * pt + 512 * mh:
                                              1024 * pt + 512 * (mh + 1)],
                                   in0=ob, scalar=sc[:, 4 * pt + 1:4 * pt + 2],
                                   in1=stream[:, 1024 * pt + 512 * mh:
                                              1024 * pt + 512 * (mh + 1)],
                                   op0=ALU.mult, op1=ALU.add), [(DV, 1)])
                        oguard[li % 4] = S.cnt[DV]
                        if mh == 1:
                            stream_done[pt] = S.cnt[DV]

                    if mh == 0:
                        S.wait("tensor", LD, ld_wd)
                        for j in range(8):
                            S.wait("tensor", DV, rh_done[4 * j + 3])
                            for li in range(4):
                                ob = obank4[li]
                                for ic in range(4 * j, 4 * j + 4):
                                    S.emit("tensor", lambda e, ic=ic, li=li,
                                           ob=ob: e.matmul(
                                               out=ob,
                                               lhsT=hF[:, 1024 * ic + 128 * li:
                                                       1024 * ic + 128 * (li + 1)],
                                               rhs=wdb[:, 512 * ic:512 * (ic + 1)],
                                               start=(ic == 0),
                                               stop=(ic == 4 * j + 3),
                                               skip_group_check=True),
                                           [(PE, 1)] if (j == 7 and li == 3 and
                                                         ic == 31) else [])
                        pe_o = S.cnt[PE]
                        S.wait("vector", PE, pe_o)
                        for li in range(4):
                            emit_ostt(li, pts[li], obank4[li], 0)
                    for li, pt in enumerate(pts):
                        if mh == 0 and li < 4:
                            continue
                        slot = li % 4
                        ob = obank4[slot]
                        S.wait("tensor", LD, ld_wd)
                        if slot in oguard:
                            S.wait("tensor", DV, oguard[slot])
                        for ic in range(32):
                            S.emit("tensor", lambda e, ic=ic, li=li, ob=ob: e.matmul(
                                out=ob,
                                lhsT=hF[:, 1024 * ic + 128 * li:1024 * ic + 128 * (li + 1)],
                                rhs=wdb[:, 512 * ic:512 * (ic + 1)],
                                start=(ic == 0), stop=(ic == 31)),
                                [(PE, 1)] if ic == 31 else [])
                        pe_o = S.cnt[PE]
                        S.wait("vector", PE, pe_o)
                        emit_ostt(li, pt, ob, mh)

                if tg + 1 < NTG:
                    v_rxT = ln_pass4(tg + 1, pe_sbc)

        emit_ffn(0)
        S.barrier()
        if dbg:
            for pt in range(NPT):
                S.emit("gpsimd", lambda e, pt=pt: e.dma_start(
                    out=dbg_d["dbg_x1"].rearrange("(n p) m -> n p m", p=128)[pt],
                    in_=stream[:, 1024 * pt:1024 * (pt + 1)]), [(ST, 16)])
            S.barrier()

        LLv3 = lnsc[:, :].rearrange("p (l c) -> p l c", c=16)
        for half in range(2):
            hpts = list(range(8 * half, 8 * half + 8))
            S.wait("vector", PE, S.cnt[PE])
            S.vsync()
            for li, pt in enumerate(hpts):
                c0 = 16 * li
                xs = stream[:, 1024 * pt:1024 * (pt + 1)]
                S.emit("vector", lambda e, c0=c0, xs=xs: e.bn_stats(
                    out=lnsc[:, c0:c0 + 6], in_=xs[:, 0:512]), [(DV, 1)])
                S.emit("vector", lambda e, c0=c0, xs=xs: e.bn_stats(
                    out=lnsc[:, c0 + 6:c0 + 12], in_=xs[:, 512:1024]), [(DV, 1)])
            S.vsync()
            for li in range(8):
                c0 = 16 * li
                S.emit("vector", lambda e, c0=c0: e.bn_aggr(
                    out=lnsc[:, c0 + 14:c0 + 16], in_=lnsc[:, c0:c0 + 12]), [(DV, 1)])
            S.vsync()
            S.wait("scalar", DV, S.cnt[DV])
            S.emit("scalar", lambda e: e.activation(
                out=smallf[:, 0:8], in_=LLv3[:, 0:8, 15:16], func=AF.Ln,
                bias=epsb[:]), [(AC, 1)])
            S.acsync()
            S.emit("scalar", lambda e: e.activation(
                out=smallf[:, 0:8], in_=smallf[:, 0:8], func=AF.Exp, scale=-0.5),
                [(AC, 1)])
            S.acsync()
            S.wait("vector", AC, S.cnt[AC])
            S.emit("vector", lambda e: e.scalar_tensor_tensor(
                out=qv[:, 0:8], in0=LLv3[:, 0:8, 14:15], scalar=-1.0,
                in1=smallf[:, 0:8], op0=ALU.mult, op1=ALU.mult), [(DV, 1)])
            v_nm = S.cnt[DV]
            for li, pt in enumerate(hpts):
                xs = stream[:, 1024 * pt:1024 * (pt + 1)]
                S.wait("scalar", DV, v_nm)
                S.wait("scalar", PE, S.cnt[PE])
                S.emit("scalar", lambda e, xs=xs, li=li: e.activation(
                    out=xhat3[:], in_=xs, func=AF.Identity,
                    bias=qv[:, li:li + 1], scale=smallf[:, li:li + 1]), [(AC, 1)])
                a_ln = S.cnt[AC]
                S.wait("tensor", AC, a_ln)
                for bq in range(2):
                    S.wait("tensor", DV, S.cnt[DV])
                    for j in range(4):
                        S.emit("tensor", lambda e, bq=bq, j=j: e.transpose(
                            out=P7b[:, 128 * j:128 * (j + 1)],
                            in_=xhat3[:, 128 * (4 * bq + j):128 * (4 * bq + j + 1)],
                            identity=ident_b[:]), [(PE, 1)] if j == 3 else [])
                    S.wait("vector", PE, S.cnt[PE])
                    S.emit("vector", lambda e, bq=bq, pt=pt: e.tensor_copy(
                        out=R8(xats)[:, 4 * bq:4 * bq + 4, 128 * pt:128 * (pt + 1)],
                        in_=P7b[:, 0:512].rearrange("p (j c) -> p j c", j=4)), [(DV, 1)])
            xh_d = (xat0 if half == 0 else xat1).rearrange("(dc p) f -> dc p f", p=128)
            S.wait("gpsimd", DV, S.cnt[DV])
            for dc in range(8):
                S.emit("gpsimd", lambda e, dc=dc, half=half, xh_d=xh_d: e.dma_start(
                    out=xh_d[dc],
                    in_=xats[:, 2048 * dc + 1024 * half:2048 * dc + 1024 * (half + 1)]),
                    [(ST, 16)])
            S.wait("gpsimd", ST, S.cnt[ST])
            if half == 0:
                S.emit("gpsimd", lambda e: e.collective_compute(
                    "AllGather", ALU.bypass,
                    replica_groups=[[0, 1], [2, 3], [4, 5], [6, 7]],
                    ins=[xat0], outs=[Gx0]), [(CX, 1)])
            else:
                st_half1 = S.cnt[ST]
        S.barrier()
        if dbg:
            S.emit("gpsimd", lambda e: e.dma_start(
                out=dbg_d["dbg_gx"].rearrange("n (h f) -> h n f", h=2)[0], in_=Gx0),
                   [(ST, 16)])
            S.emit("gpsimd", lambda e: e.dma_start(
                out=dbg_d["dbg_gx"].rearrange("n (h f) -> h n f", h=2)[1], in_=Gx1),
                   [(ST, 16)])
            S.barrier()

        vw_t = vw_m.rearrange("(s p) f -> p s f", p=128)
        S.emit("gpsimd", lambda e: e.dma_start(
            out=vwraw[:, :].rearrange("p (s f) -> p s f", s=4), in_=vw_t), [(LG, 16)])
        S.emit("gpsimd", lambda e: e.dma_start(out=vbrow[:], in_=vb_m), [(LG, 16)])
        S.emit("gpsimd", lambda e: e.dma_start(out=obrow[:], in_=ob_in), [(LG, 16)])
        S.wait("tensor", LG, S.cnt[LG])
        for s in range(4):
            for bq in range(2):
                S.wait("tensor", DV, S.cnt[DV])
                for j in range(4):
                    dc = 4 * bq + j
                    S.emit("tensor", lambda e, s=s, dc=dc, j=j: e.transpose(
                        out=P7b[:, 128 * j:128 * (j + 1)],
                        in_=vwraw[:, 1024 * s + 128 * dc:1024 * s + 128 * (dc + 1)],
                        identity=ident_b[:]), [(PE, 1)] if j == 3 else [])
                S.wait("vector", PE, S.cnt[PE])
                S.emit("vector", lambda e, s=s, bq=bq: e.tensor_copy(
                    out=R8(vwT)[:, 4 * bq:4 * bq + 4, 128 * s:128 * (s + 1)],
                    in_=P7b[:, 0:512].rearrange("p (j c) -> p j c", j=4)), [(DV, 1)])
        v_vwT = S.cnt[DV]
        S.emit("vector", lambda e: e.memset(
            VA[:, :].rearrange("p (kc h c) -> p kc h c", kc=32, h=8)[:, :, :, 64:65],
            1.0), [(DV, 1)])

        Gx0_t = Gx0.rearrange("(o dc p) f -> o p dc f", dc=8, p=128)
        Gx1_t = Gx1.rearrange("(o dc p) f -> o p dc f", dc=8, p=128)

        def emit_xw_load(tt, buf):
            o, col = tt // 4, tt % 4
            if col < 2:
                S.wait("sync", CX, 1)
                gsrc = Gx0_t[o, :, :, 512 * col:512 * (col + 1)]
            else:
                S.wait("sync", CU, 1)
                gsrc = Gx1_t[o, :, :, 512 * (col - 2):512 * (col - 1)]
            S.emit("sync", lambda e, gsrc=gsrc, buf=buf: e.dma_start(
                out=R8(xw[buf]), in_=gsrc), [(LD, 16)])
            return S.cnt[LD]

        vring = [P01[:, 0:512], P01[:, 512:1024], P23[:, 0:512], P23[:, 512:1024]]
        xw_guard = [0, 0]
        vpe = {}
        vev = {}
        kc_seq = []
        ld_xwd = {}
        tt_order = [0, 1, 4, 5, 2, 3, 6, 7]
        ld_xwd[tt_order[0]] = emit_xw_load(tt_order[0], 0)
        ld_xwd[tt_order[1]] = emit_xw_load(tt_order[1], 1)
        for ti in range(8):
            tt = tt_order[ti]
            buf = ti % 2
            S.wait("tensor", LD, ld_xwd[tt])
            if ti == 0:
                S.wait("tensor", DV, v_vwT)
            for sub in range(4):
                kc = 4 * tt + sub
                slot = kc % 4
                vps = vring[slot]
                if slot in vev:
                    S.wait("tensor", DV, vev[slot])
                for dc in range(8):
                    S.emit("tensor", lambda e, buf=buf, sub=sub, dc=dc, vps=vps:
                           e.matmul(out=vps,
                                    lhsT=xw[buf][:, 512 * dc + 128 * sub:
                                                 512 * dc + 128 * (sub + 1)],
                                    rhs=vwT[:, 512 * dc:512 * (dc + 1)],
                                    start=(dc == 0), stop=False), [])
                S.emit("tensor", lambda e, vps=vps: e.matmul(
                    out=vps, lhsT=ones_b[0:1, :], rhs=vbrow[0:1, :],
                    start=False, stop=True), [(PE, 1)])
                vpe[kc] = S.cnt[PE]
                kc_seq.append(kc)
                if len(kc_seq) >= 3:
                    k2 = kc_seq[-3]
                    S.wait("vector", PE, vpe[k2])
                    S.emit("vector", lambda e, kc=k2, vps=vring[k2 % 4]: e.tensor_copy(
                        out=VA[:, :].rearrange("p (kc h c) -> p kc h c", kc=32, h=8)[
                            :, kc, :, 0:64],
                        in_=vps.rearrange("p (h c) -> p h c", h=8)), [(DV, 1)])
                    vev[k2 % 4] = S.cnt[DV]
            xw_guard[buf] = S.cnt[PE]
            if ti + 2 < 8:
                S.wait("sync", PE, xw_guard[buf])
                ld_xwd[tt_order[ti + 2]] = emit_xw_load(tt_order[ti + 2], buf)
            if ti == 1:
                S.wait("gpsimd", ST, st_half1)
                S.wait("gpsimd", LD, S.cnt[LD])
                S.emit("gpsimd", lambda e: e.collective_compute(
                    "AllGather", ALU.bypass,
                    replica_groups=[[0, 1], [2, 3], [4, 5], [6, 7]],
                    ins=[xat1], outs=[Gx1]), [(CU, 1)])
        for k2 in kc_seq[-2:]:
            S.wait("vector", PE, vpe[k2])
            S.emit("vector", lambda e, kc=k2, vps=vring[k2 % 4]: e.tensor_copy(
                out=VA[:, :].rearrange("p (kc h c) -> p kc h c", kc=32, h=8)[
                    :, kc, :, 0:64],
                in_=vps.rearrange("p (h c) -> p h c", h=8)), [(DV, 1)])
        S.barrier()

        S.emit("gpsimd", lambda e: e.dma_start(
            out=qa_sb[:, :].rearrange("p (hp dc c) -> p hp dc c", hp=4, dc=8),
            in_=qa_in.rearrange("hp dc p c -> p hp dc c")), [(LG, 16)])
        S.emit("gpsimd", lambda e: e.dma_start(
            out=ka_sb[:, :].rearrange("p (hp dc c) -> p hp dc c", hp=4, dc=8),
            in_=ka_in.rearrange("hp dc p c -> p hp dc c")), [(LG, 16)])
        S.emit("gpsimd", lambda e: e.dma_start(
            out=qb_sb[:, :].rearrange("p (hp c) -> p hp c", hp=4),
            in_=qb_in.rearrange("hp p c -> p hp c")), [(LG, 16)])
        S.emit("gpsimd", lambda e: e.dma_start(
            out=kb_sb[:, :].rearrange("p (hp c) -> p hp c", hp=4),
            in_=kb_in.rearrange("hp p c -> p hp c")), [(LG, 16)])
        S.wait("tensor", LG, S.cnt[LG])

        units = [(tt, hp, w) for tt in range(8) for hp in range(4) for w in range(2)]
        mring = [P01[:, 0:512], P01[:, 512:1024], P23[:, 0:512], P23[:, 512:1024]]
        qring = [P45[:, 0:512], P45[:, 512:1024]]
        xw_guard = [0, 0]
        pe_mid = {}
        ev_mid = {}
        pe_qb = {}
        ev_qt = {}
        ld_tt = {}
        NU = len(units)

        def u_parts(i):
            tt, hp, w = units[i]
            asb = qa_sb if w == 0 else ka_sb
            bsb = qb_sb if w == 0 else kb_sb
            dstT = QT if w == 0 else KT
            return tt, hp, w, asb, bsb, dstT

        for i in range(NU + 2):
            if i < NU:
                tt, hp, w, asb, bsb, dstT = u_parts(i)
                buf = tt % 2
                if i == 0:
                    ld_tt[0] = emit_xw_load(0, 0)
                    ld_tt[1] = emit_xw_load(1, 1)
                S.wait("tensor", LD, ld_tt[tt])
                ms = i % 4
                if ms in ev_mid:
                    S.wait("tensor", DV, ev_mid[ms])
                for dc in range(8):
                    S.emit("tensor", lambda e, asb=asb, hp=hp, dc=dc, buf=buf,
                           mps=mring[ms]: e.matmul(
                               out=mps,
                               lhsT=asb[:, 1024 * hp + 128 * dc:
                                        1024 * hp + 128 * (dc + 1)],
                               rhs=xw[buf][:, 512 * dc:512 * (dc + 1)],
                               start=(dc == 0), stop=(dc == 7)),
                           [(PE, 1)] if dc == 7 else [])
                pe_mid[i] = S.cnt[PE]
                if hp == 3 and w == 1:
                    xw_guard[buf] = S.cnt[PE]
                    if tt + 2 < 8:
                        S.wait("sync", PE, xw_guard[buf])
                        ld_tt[tt + 2] = emit_xw_load(tt + 2, buf)
                S.wait("vector", PE, pe_mid[i])
                if i - 4 in pe_qb:
                    S.wait("vector", PE, pe_qb[i - 4])
                S.emit("vector", lambda e, mb=mids[i % 4], mps=mring[ms]: e.tensor_copy(
                    out=mb[:], in_=mps), [(DV, 1)])
                ev_mid[ms] = S.cnt[DV]
            if i >= 2:
                j = i - 2
                tt, hp, w, asb, bsb, dstT = u_parts(j)
                qs = j % 2
                if qs in ev_qt:
                    S.wait("tensor", DV, ev_qt[qs])
                S.wait("tensor", DV, ev_mid[j % 4])
                S.emit("tensor", lambda e, bsb=bsb, hp=hp, mb=mids[j % 4],
                       qps=qring[qs]: e.matmul(
                           out=qps, lhsT=bsb[:, 128 * hp:128 * (hp + 1)],
                           rhs=mb[:], start=True, stop=True), [(PE, 1)])
                pe_qb[j] = S.cnt[PE]
                S.wait("vector", PE, pe_qb[j])
                S.emit("vector", lambda e, dstT=dstT, hp=hp, tt=tt, qps=qring[qs]:
                       e.tensor_copy(
                           out=dstT[:, 4096 * hp + 512 * tt:
                                    4096 * hp + 512 * (tt + 1)],
                           in_=qps), [(DV, 1)])
                ev_qt[qs] = S.cnt[DV]
        S.barrier()
        if dbg:
            for hp in range(4):
                S.emit("gpsimd", lambda e, hp=hp: e.dma_start(
                    out=dbg_d["dbg_qt"][hp],
                    in_=QT[:, 4096 * hp:4096 * (hp + 1)]), [(ST, 16)])
                S.emit("gpsimd", lambda e, hp=hp: e.dma_start(
                    out=dbg_d["dbg_kt"][hp],
                    in_=KT[:, 4096 * hp:4096 * (hp + 1)]), [(ST, 16)])
            S.emit("gpsimd", lambda e: e.dma_start(
                out=dbg_d["dbg_v"], in_=VA[:, 0:16640]), [(ST, 16)])
            S.barrier()

        S.emit("gpsimd", lambda e: e.dma_start(
            out=msk[:, :].rearrange("p (o c) -> p o c", o=4),
            in_=mask_in.rearrange("o p c -> p o c")), [(LG, 16)])
        ow_t = ow_m.rearrange("(s p) f -> p s f", p=128)
        S.emit("gpsimd", lambda e: e.dma_start(
            out=owraw[:, :].rearrange("p (s f) -> p s f", s=8), in_=ow_t), [(LG, 16)])
        lg_ow = S.cnt[LG]
        S.wait("tensor", LG, lg_ow)
        for s in range(8):
            S.wait("tensor", DV, S.cnt[DV])
            for j in range(4):
                S.emit("tensor", lambda e, s=s, j=j: e.transpose(
                    out=P7b[:, 128 * j:128 * (j + 1)],
                    in_=owraw[:, 512 * s + 128 * j:512 * s + 128 * (j + 1)],
                    identity=ident_b[:]), [(PE, 1)] if j == 3 else [])
            S.wait("vector", PE, S.cnt[PE])
            S.emit("vector", lambda e, s=s: e.tensor_copy(
                out=owT[:, :].rearrange("p (hp c) -> p hp c", hp=4)[
                    :, :, 128 * s:128 * (s + 1)],
                in_=P7b[:, 0:512].rearrange("p (j c) -> p j c", j=4)), [(DV, 1)])

        op_t = opart.rearrange("(k a sub p) f -> k a sub p f", a=2, sub=4, p=128)
        op_v = opart.rearrange("(k r) f -> k r f", k=4)

        ors_t = [orsl[k].rearrange("(n p) m -> n p m", p=128) for k in range(4)]
        attn_add_guard = [0]
        p45_free = [0]
        s_banks = [P01, P23]
        exp_done = [0, 0]
        avc_done = [0, 0]
        mskbuf_done = [0, 0]
        opst_guard = [0, 0]
        g_done = set()
        cc_o = {}
        for g in (0, 4, 1, 5, 2, 6, 3, 7):
            for hp in range(4):
                if g == 6 and not dbg:
                    for pt in (2 * hp, 2 * hp + 1):
                        S.wait("sync", CRk[pt // 4], 1)
                        if attn_add_guard[0]:
                            S.wait("sync", DV, attn_add_guard[0])
                        S.emit("sync", lambda e, pt=pt: e.dma_start(
                            out=xoa[:], in_=ors_t[pt // 4][pt % 4]), [(CD, 16)])
                        S.wait("vector", CD, S.cnt[CD])
                        S.emit("vector", lambda e, pt=pt: e.tensor_add(
                            out=stream[:, 1024 * pt:1024 * (pt + 1)],
                            in0=stream[:, 1024 * pt:1024 * (pt + 1)], in1=xoa[:]),
                            [(DV, 1)])
                        attn_add_guard[0] = S.cnt[DV]
                nkc = 4 * (g + 1)
                nch = (nkc + 1) // 2
                yA = P45[0:65, 0:512]
                yB = P45[0:65, 512:1024]
                exp_in_ch = {}
                msk_in_ch = {}

                def emit_av(c, nkc=nkc, g=g, hp=hp):
                    if c == 0:
                        S.wait("tensor", DV, p45_free[0])
                    S.wait("tensor", AC, exp_in_ch[c])
                    if c in msk_in_ch:
                        S.wait("tensor", DV, msk_in_ch[c])
                    for uu in range(2):
                        kc = 2 * c + uu
                        if kc >= nkc:
                            continue
                        for hh in range(2):
                            lh = 2 * hp + hh
                            yps = yA if hh == 0 else yB
                            S.emit("tensor", lambda e, kc=kc, lh=lh, hh=hh, c=c, uu=uu,
                                   yps=yps, first=(kc == 0), last=(kc == nkc - 1):
                                   e.matmul(
                                       out=yps,
                                       lhsT=VA[:, 520 * kc + 65 * lh:
                                               520 * kc + 65 * (lh + 1)],
                                       rhs=pbuf[c % 2][:, 1024 * uu + 512 * hh:
                                                       1024 * uu + 512 * (hh + 1)],
                                       start=first, stop=last, skip_group_check=True),
                                   [(PE, 1)] if (hh == 1 and (kc == nkc - 1 or uu == 1))
                                   else [])
                    avc_done[c % 2] = S.cnt[PE]

                for c in range(nch):
                    for uu in range(2):
                        kc = 2 * c + uu
                        if kc >= nkc:
                            continue
                        par = kc % 2
                        sb = s_banks[par]
                        S.wait("tensor", AC, exp_done[par])
                        for hh in range(2):
                            base = 64 * hh
                            S.emit("tensor", lambda e, hp=hp, kc=kc, base=base:
                                   e.ldweights(
                                       KT[base:base + 64,
                                          4096 * hp + 128 * kc:4096 * hp + 128 * (kc + 1)],
                                       tile_position=(base, 0)), [])
                            S.emit("tensor", lambda e, hp=hp, kc=kc, g=g, hh=hh,
                                   base=base, sb=sb: e.matmul(
                                       out=sb[:, 512 * hh:512 * (hh + 1)],
                                       lhsT=KT[base:base + 64,
                                               4096 * hp + 128 * kc:
                                               4096 * hp + 128 * (kc + 1)],
                                       rhs=QT[base:base + 64,
                                              4096 * hp + 512 * g:
                                              4096 * hp + 512 * (g + 1)],
                                       start=True, stop=True, skip_group_check=True,
                                       tile_position=(base, 0)),
                                   [(PE, 1)] if hh == 1 else [])
                        pe_s = S.cnt[PE]
                        S.wait("scalar", PE, pe_s)
                        S.wait("scalar", PE, avc_done[c % 2])
                        S.wait("scalar", DV, mskbuf_done[c % 2])
                        S.emit("scalar", lambda e, c=c, uu=uu, sb=sb: e.activation(
                            out=pbuf[c % 2][:, 1024 * uu:1024 * (uu + 1)], in_=sb[:],
                            func=AF.Exp, scale=0.125, bias=shiftb[:]), [(AC, 1)])
                        exp_done[par] = S.cnt[AC]
                        exp_in_ch[c] = S.cnt[AC]
                        o = kc - 4 * g
                        if o >= 0:
                            S.wait("vector", AC, exp_in_ch[c])
                            S.wait("vector", LG, lg_ow)
                            for hh in range(2):
                                S.emit("vector", lambda e, c=c, uu=uu, o=o, hh=hh:
                                       e.tensor_mul(
                                           out=pbuf[c % 2][:, 1024 * uu + 512 * hh:
                                                           1024 * uu + 512 * (hh + 1)],
                                           in0=pbuf[c % 2][:, 1024 * uu + 512 * hh:
                                                           1024 * uu + 512 * (hh + 1)],
                                           in1=msk[:, 512 * o:512 * (o + 1)]),
                                       [(DV, 1)])
                            msk_in_ch[c] = S.cnt[DV]
                            mskbuf_done[c % 2] = S.cnt[DV]
                    if c >= 1:
                        emit_av(c - 1)
                emit_av(nch - 1)
                pe_av = S.cnt[PE]
                S.wait("vector", PE, pe_av)
                S.emit("vector", lambda e, yA=yA: e.reciprocal(
                    out=nsc[0:1, 0:512], in_=yA[64:65, :]), [(DV, 1)])
                S.emit("vector", lambda e, yB=yB: e.reciprocal(
                    out=nsc[0:1, 512:1024], in_=yB[64:65, :]), [(DV, 1)])
                v_rc = S.cnt[DV]
                S.wait("tensor", DV, v_rc)
                S.emit("tensor", lambda e: e.matmul(
                    out=P6[0:64, :], lhsT=ones_f[0:1, 0:64], rhs=nsc[0:1, 0:512],
                    start=True, stop=True), [])
                S.emit("tensor", lambda e: e.matmul(
                    out=P6[64:128, :], lhsT=ones_f[0:1, 0:64], rhs=nsc[0:1, 512:1024],
                    start=True, stop=True), [(PE, 1)])
                S.wait("vector", PE, S.cnt[PE])
                S.emit("vector", lambda e: e.tensor_copy(
                    out=nsc[:, 0:512], in_=P6[:]), [(DV, 1)])
                S.vsync()
                S.emit("vector", lambda e, hp=hp, yA=yA: e.tensor_mul(
                    out=ybuf[0:64, 512 * hp:512 * (hp + 1)], in0=yA[0:64, :],
                    in1=nsc[0:64, 0:512]), [(DV, 1)])
                S.emit("vector", lambda e, hp=hp, yB=yB: e.tensor_mul(
                    out=ybuf[64:128, 512 * hp:512 * (hp + 1)], in0=yB[0:64, :],
                    in1=nsc[64:128, 0:512]), [(DV, 1)])
                p45_free[0] = S.cnt[DV]
            v_y = S.cnt[DV]
            for sub in range(4):
                par = sub % 2
                for mh in range(2):
                    ops_ = P6[:, 0:512] if mh == 0 else P45[:, 0:512]
                    S.wait("tensor", DV, v_y)
                    S.wait("tensor", DV, S.cnt[DV])
                    for hp in range(4):
                        S.emit("tensor", lambda e, hp=hp, sub=sub, mh=mh, ops_=ops_:
                               e.matmul(
                                   out=ops_,
                                   lhsT=ybuf[:, 512 * hp + 128 * sub:
                                             512 * hp + 128 * (sub + 1)],
                                   rhs=owT[:, 1024 * hp + 512 * mh:
                                           1024 * hp + 512 * (mh + 1)],
                                   start=(hp == 0), stop=False,
                                   skip_group_check=True), [])
                    S.emit("tensor", lambda e, mh=mh, ops_=ops_: e.matmul(
                        out=ops_, lhsT=half_b[0:1, :],
                        rhs=obrow[0:1, 512 * mh:512 * (mh + 1)],
                        start=False, stop=True, skip_group_check=True), [(PE, 1)])
                    S.wait("vector", PE, S.cnt[PE])
                    S.wait("vector", ST, opst_guard[par])
                    S.emit("vector", lambda e, par=par, mh=mh, ops_=ops_: e.tensor_copy(
                        out=opst[par][:, 512 * mh:512 * (mh + 1)], in_=ops_), [(DV, 1)])
                    p45_free[0] = S.cnt[DV]
                S.wait("gpsimd", DV, S.cnt[DV])
                S.emit("gpsimd", lambda e, g=g, sub=sub, par=par: e.dma_start(
                    out=op_t[g % 4, g // 4, sub], in_=opst[par][:]), [(ST, 16)])
                opst_guard[par] = S.cnt[ST]
            g_done.add(g)
            k = g % 4
            if k in g_done and (k + 4) in g_done and k not in cc_o:
                S.wait("gpsimd", ST, S.cnt[ST])
                S.wait("gpsimd", CD, S.cnt[CD])
                S.emit("gpsimd", lambda e, k=k: e.collective_compute(
                    "ReduceScatter", ALU.add,
                    replica_groups=[[0, 1], [2, 3], [4, 5], [6, 7]],
                    ins=[op_v[k]], outs=[orsl[k]]), [(CRk[k], 1)])
                cc_o[k] = 1
        for _e in ENGS:
            for _s, _v in S.cnt.items():
                if _v > 0 and _s not in ("CR2", "CR3"):
                    S.wait(_e, _s, _v)
        S.wait("sync", CR3, 1)
        gv2 = Gall.rearrange("(r w dc p) f -> w r p dc f", w=3, dc=8, p=128)
        pre_ld2 = {}
        for c32 in (0, 1):
            r, jj, par = c32 // 4, c32 % 4, c32 % 2
            S.emit("sync", lambda e, r=r, jj=jj, par=par: e.dma_start(
                out=R8(wgb[par]), in_=gv2[0][r, :, :, 128 * jj:128 * (jj + 1)]),
                [(LD, 16)])
            S.emit("sync", lambda e, r=r, jj=jj, par=par: e.dma_start(
                out=R8(wub[par]), in_=gv2[1][r, :, :, 128 * jj:128 * (jj + 1)]),
                [(LD, 16)])
            pre_ld2[c32] = S.cnt[LD]
        if dbg:
            S.wait("gpsimd", CR2, 1)
            S.wait("gpsimd", CR3, 1)
            for k in range(4):
                S.emit("gpsimd", lambda e, k=k: e.dma_start(
                    out=dbg_d["dbg_ors"].rearrange("(b r) f -> b r f", b=4)[k],
                    in_=orsl[k]), [(ST, 16)])
            S.barrier()

        xo_guard = [0, 0]
        for pt in range(NPT if dbg else 0):
            par = pt % 2
            S.wait("sync", CR3, 1)
            S.wait("sync", CRk[pt // 4], 1)
            S.wait("sync", DV, xo_guard[par])
            S.emit("sync", lambda e, pt=pt, par=par: e.dma_start(
                out=xo[par][:], in_=ors_t[pt // 4][pt % 4]), [(LD, 16)])
            S.wait("vector", LD, S.cnt[LD])
            S.emit("vector", lambda e, pt=pt, par=par: e.tensor_add(
                out=stream[:, 1024 * pt:1024 * (pt + 1)],
                in0=stream[:, 1024 * pt:1024 * (pt + 1)], in1=xo[par][:]), [(DV, 1)])
            xo_guard[par] = S.cnt[DV]
        S.barrier()

        def ph6_tail_hook(pe_guard):
            g2 = [0, 0]
            S.wait("sync", PE, pe_guard)
            S.wait("sync", CR3, 1)
            for pt in range(8, NPT):
                par = pt % 2
                if g2[par]:
                    S.wait("sync", DV, g2[par])
                S.emit("sync", lambda e, pt=pt, par=par: e.dma_start(
                    out=xo2[par][:], in_=ors_t[pt // 4][pt % 4]), [(CD, 16)])
                S.wait("vector", CD, S.cnt[CD])
                S.emit("vector", lambda e, pt=pt, par=par: e.tensor_add(
                    out=stream[:, 1024 * pt:1024 * (pt + 1)],
                    in0=stream[:, 1024 * pt:1024 * (pt + 1)], in1=xo2[par][:]),
                    [(DV, 1)])
                g2[par] = S.cnt[DV]
        if dbg:
            for pt in range(NPT):
                S.emit("gpsimd", lambda e, pt=pt: e.dma_start(
                    out=dbg_d["dbg_x2"].rearrange("(n p) m -> n p m", p=128)[pt],
                    in_=stream[:, 1024 * pt:1024 * (pt + 1)]), [(ST, 16)])
            S.barrier()

        emit_ffn(1, mid_hook=None if dbg else ph6_tail_hook, pre_ld=pre_ld2)
        out_t = out_d.rearrange("(n p) m -> n p m", p=128)
        for pt in range(NPT):
            S.wait("gpsimd", DV, stream_done[pt])
            S.emit("gpsimd", lambda e, pt=pt: e.dma_start(
                out=out_t[pt], in_=stream[:, 1024 * pt:1024 * (pt + 1)]), [(ST, 16)])
        S.barrier()
        S.wait("gpsimd", ST, S.cnt[ST])

        S.simulate()
        with nc.Block() as block:
            @block.sync
            def _(e):
                S.replay("sync", e, sems)

            @block.tensor
            def _(e):
                S.replay("tensor", e, sems)

            @block.vector
            def _(e):
                S.replay("vector", e, sems)

            @block.scalar
            def _(e):
                S.replay("scalar", e, sems)

            @block.gpsimd
            def _(e):
                S.replay("gpsimd", e, sems)

    return nc


def _host_inputs(x, q_a, q_b, k_a, k_b, v_w, v_b, o_w, o_b, w_gate, w_up, w_down):
    in_maps = []
    masks = np.zeros((4, 128, 512), np.float32)
    qi = np.arange(512)[None, :]
    ki = np.arange(128)[:, None]
    for o in range(4):
        masks[o] = (qi - ki >= 128 * o).astype(np.float32)
    ident = np.eye(128, dtype=np.float32)
    for c in range(8):
        b, p = c // 2, c % 2
        qa_m = q_a[8 * p:8 * p + 8]
        ka_m = k_a[8 * p:8 * p + 8]
        qa_l = np.ascontiguousarray(
            qa_m.reshape(4, 2, 8, 128, 64).transpose(0, 2, 3, 1, 4).reshape(4, 8, 128, 128))
        ka_l = np.ascontiguousarray(
            ka_m.reshape(4, 2, 8, 128, 64).transpose(0, 2, 3, 1, 4).reshape(4, 8, 128, 128))
        qb_m = q_b[8 * p:8 * p + 8]
        kb_m = k_b[8 * p:8 * p + 8]
        qb_l = np.zeros((4, 128, 128), np.float32)
        kb_l = np.zeros((4, 128, 128), np.float32)
        qb_l[:, 0:64, 0:64] = qb_m[0::2]
        qb_l[:, 64:128, 64:128] = qb_m[1::2]
        kb_l[:, 0:64, 0:64] = kb_m[0::2]
        kb_l[:, 64:128, 64:128] = kb_m[1::2]
        in_maps.append({
            "x": np.ascontiguousarray(x[b, 2048 * p:2048 * (p + 1)]),
            "wg_sh": np.ascontiguousarray(w_gate[512 * c:512 * (c + 1)]),
            "wu_sh": np.ascontiguousarray(w_up[512 * c:512 * (c + 1)]),
            "wd_sh": np.ascontiguousarray(w_down[:, 512 * c:512 * (c + 1)]),
            "vw_m": np.ascontiguousarray(v_w[512 * p:512 * (p + 1)]),
            "ow_m": np.ascontiguousarray(o_w[:, 512 * p:512 * (p + 1)]),
            "vb_m": np.ascontiguousarray(v_b[512 * p:512 * (p + 1)].reshape(1, 512)),
            "ob": np.ascontiguousarray(o_b.reshape(1, 1024)),
            "qa_l": qa_l, "ka_l": ka_l, "qb_l": qb_l, "kb_l": kb_l,
            "masks": masks, "ident": ident,
        })
    return in_maps


_NC_CACHE = {}


def kernel(L0_attn_hidden_states, L0_attn_q_a, L0_attn_q_b, L0_attn_k_a,
           L0_attn_k_b, L0_attn_v_w, L0_attn_v_b, L0_attn_o_w, L0_attn_o_b,
           L0_ffn_w_gate, L0_ffn_w_up, L0_ffn_w_down, _dbg=False):
    x = np.asarray(L0_attn_hidden_states, np.float32)
    in_maps = _host_inputs(
        x, np.asarray(L0_attn_q_a, np.float32), np.asarray(L0_attn_q_b, np.float32),
        np.asarray(L0_attn_k_a, np.float32), np.asarray(L0_attn_k_b, np.float32),
        np.asarray(L0_attn_v_w, np.float32), np.asarray(L0_attn_v_b, np.float32),
        np.asarray(L0_attn_o_w, np.float32), np.asarray(L0_attn_o_b, np.float32),
        np.asarray(L0_ffn_w_gate, np.float32), np.asarray(L0_ffn_w_up, np.float32),
        np.asarray(L0_ffn_w_down, np.float32))
    key = bool(_dbg)
    if key not in _NC_CACHE:
        _NC_CACHE[key] = build(dbg=_dbg)
    res = run_bass_kernel_spmd(_NC_CACHE[key], in_maps, core_ids=list(range(8)))
    out = np.empty((4, 4096, 1024), np.float32)
    for c in range(8):
        b, p = c // 2, c % 2
        out[b, 2048 * p:2048 * (p + 1)] = res.results[c]["out"]
    if _dbg:
        return out, res
    return out
